# version 03
# baseline (speedup 1.0000x reference)
import numpy as np
import ml_dtypes
import concourse.bass as bass
import concourse.mybir as mybir
from concourse.bass_utils import run_bass_kernel_spmd

F32 = mybir.dt.float32
BF16 = mybir.dt.bfloat16
AF = mybir.ActivationFunctionType
ALU = mybir.AluOpType
AX = mybir.AxisListType
BF16NP = ml_dtypes.bfloat16

NCORES = 8
T = 1024
S = 8192
D = 2048
FF = 5632
NFC = 44
NDC = 16
NPAIR = 15
CSH = 20.0
NEG = -1.0e9
EPS_LN = 1.0e-5
EPS_Q = 1.0e-8
DEBUG = False


def build_nc():
    nc = bass.Bass(target_bir_lowering=False)

    d_xhal = nc.dram_tensor("x_hal", [1536, D], F32, kind="ExternalInput")
    d_xg = nc.dram_tensor("x_glob", [S, D], F32, kind="ExternalInput")
    d_wq = nc.dram_tensor("wq_t", [16, 128, D], F32, kind="ExternalInput")
    d_wk = nc.dram_tensor("wk_t", [16, 128, D], F32, kind="ExternalInput")
    d_wv = nc.dram_tensor("wv_t", [16, 128, D], F32, kind="ExternalInput")
    d_wo = nc.dram_tensor("wo_t", [16, 128, D], F32, kind="ExternalInput")
    d_bo = nc.dram_tensor("bo_bc", [128, D], F32, kind="ExternalInput")
    d_w1 = nc.dram_tensor("w1_t", [NFC, 128, D], F32, kind="ExternalInput")
    d_w2 = nc.dram_tensor("w2_t", [NFC, 128, D], F32, kind="ExternalInput")
    d_w3 = nc.dram_tensor("w3_t", [NFC, 128, D], F32, kind="ExternalInput")
    d_cl = nc.dram_tensor("cos_loc", [128, 1536], BF16, kind="ExternalInput")
    d_sl = nc.dram_tensor("sin_loc", [128, 1536], BF16, kind="ExternalInput")
    d_cg = nc.dram_tensor("cos_glob", [128, S], BF16, kind="ExternalInput")
    d_sg = nc.dram_tensor("sin_glob", [128, S], BF16, kind="ExternalInput")
    d_msk = nc.dram_tensor("masks", [128, 2048], BF16, kind="ExternalInput")
    d_bl = nc.dram_tensor("bias_loc", [128, 24], F32, kind="ExternalInput")
    d_bg = nc.dram_tensor("bias_glob", [128, 128], F32, kind="ExternalInput")
    d_id = nc.dram_tensor("identb", [128, 128], BF16, kind="ExternalInput")
    d_id64 = nc.dram_tensor("id64", [128, 64], BF16, kind="ExternalInput")
    d_P = nc.dram_tensor("ropeP", [128, 128], BF16, kind="ExternalInput")
    d_y = nc.dram_tensor("y", [T, D], F32, kind="ExternalOutput")
    dbg_tensors = {}

    from contextlib import ExitStack
    es = ExitStack()
    with es:
        E = es.enter_context
        AB = E(nc.sbuf_tensor("AB", [128, 45056], BF16))
        XA = E(nc.sbuf_tensor("XA", [128, 24576], BF16))
        MID = E(nc.sbuf_tensor("MID", [128, 16384], BF16))
        WQS = E(nc.sbuf_tensor("WQS", [128, 8192], BF16))
        XS = E(nc.sbuf_tensor("XS", [128, 2048], F32))
        AUX = E(nc.sbuf_tensor("AUX", [128, 7168], BF16))
        BIA = E(nc.sbuf_tensor("BIA", [128, 152], F32))
        ONE = E(nc.sbuf_tensor("ONE", [128, 64], F32))
        IDE = E(nc.sbuf_tensor("IDE", [128, 128], BF16))
        ID64 = E(nc.sbuf_tensor("ID64", [128, 64], BF16))
        RP = E(nc.sbuf_tensor("RP", [128, 128], BF16))
        STT = E(nc.sbuf_tensor("STT", [128, 24], F32))

        MID32 = MID.bitcast(F32)
        AUX32 = AUX.bitcast(F32)
        XA32 = XA.bitcast(F32)

        attn_fm = lambda p: AB[:, p * 1024:(p + 1) * 1024]

        def attn_fm_h(p, hb, c0, c1):
            return AB[hb:hb + 64, p * 1024 + c0: p * 1024 + c1]

        KG0 = 16384
        Kg = lambda c0, c1: AB[:, KG0 + c0:KG0 + c1]
        VG0 = KG0 + 8192
        Vg3 = AB[:, VG0:VG0 + 128 * 65].rearrange("p (k c) -> p k c", c=65)
        Vg_t = lambda kt: AB[:, VG0 + kt * 65: VG0 + kt * 65 + 65]
        XG0 = VG0 + 128 * 65
        xg_fm = lambda dc: AB[:, XG0 + dc * 256: XG0 + (dc + 1) * 256]
        xg3 = AB[:, XG0:XG0 + 4096].rearrange("p (d t) -> p d t", d=16)
        KVW0 = XG0 + 16 * 256
        kvgw = lambda m: AB[:, KVW0 + m * 2048: KVW0 + (m + 1) * 2048]
        ABS0 = KVW0 + 2 * 2048
        absbuf = lambda i: AB[:, ABS0 + i * 704: ABS0 + (i + 1) * 704]
        wo_ab = lambda i: AB[:, 16384 + i * 2048: 16384 + (i + 1) * 2048]
        hid = lambda j: AB[:, j * 1024:(j + 1) * 1024]

        xln = lambda dc, c0, c1: XA[:, dc * 1536 + c0: dc * 1536 + c1]
        xa3 = XA[:, 0:24576].rearrange("p (d t) -> p d t", d=16)
        r1t = lambda tc: XA[:, tc * 2048:(tc + 1) * 2048]
        XLT5 = XA[:, 16384:18432]
        JUNK5 = XA[:, 18432:20480]
        QT1 = XA[:, 20480:22528]
        QT2 = XA[:, 22528:24576]
        TMPF6 = XA32[:, 10240:12288]
        TMP6 = XA[:, 16384:16896]
        JUNK6 = XA[:, 18432:20480]

        def expsl(hb, kind, par):
            off = (0 if hb == 0 else 4096) + kind * 2048 + par * 1024
            return MID[:, off: off + 1024]

        QP0, KP0, RW0, RT0, VF0 = 8192, 9216, 10752, 11776, 12800
        Qp = lambda c0, c1: MID[:, QP0 + c0:QP0 + c1]
        Kp = lambda c0, c1: MID[:, KP0 + c0:KP0 + c1]
        RAW0 = MID[:, RW0:RW0 + 512]
        RAW1 = MID[:, RW0 + 512:RW0 + 1024]
        RTMP = MID[:, RT0:RT0 + 512]
        VFM = MID[:, VF0:VF0 + 1536]
        XLT = MID[:, 0:2048]
        JUNK = MID[:, 2048:4096]
        APROJ = MID32[:, 0:2048]
        BOBC = MID32[:, 2048:4096]
        TMPF = MID32[:, 4096:6144]
        XS_M1 = MID32[:, 2048:4096]
        XS_M4 = MID32[:, 6144:8192]
        mid3 = MID.rearrange("p (d t) -> p d t", d=16)
        xln2 = lambda dc, c0, c1: MID[:, dc * 1024 + c0: dc * 1024 + c1]
        acct = lambda tc: MID[:, tc * 2048:(tc + 1) * 2048]

        wslot = lambda i: WQS[:, (i % 4) * 2048:((i % 4) + 1) * 2048]
        wo_w = lambda i: WQS[:, (i - 13) * 2048: (i - 12) * 2048]
        w12q = lambda m, par: WQS[:, (m * 2 + par) * 2048:(m * 2 + par + 1) * 2048]
        wqs_w3 = lambda i: WQS[:, i * 2048:(i + 1) * 2048]

        XSS = XS[:, 0:2048]
        SG = lambda th: AUX[:, 4096 + th * 512: 4096 + (th + 1) * 512]

        CL = AUX[:, 0:1536]
        SL = AUX[:, 1536:3072]
        cgs = lambda par: AUX[:, par * 256:(par + 1) * 256]
        sgs = lambda par: AUX[:, 1536 + par * 256: 1536 + (par + 1) * 256]
        MSK2 = lambda o: AUX[:, 3072 + o * 512: 3072 + (o + 2) * 512]
        SCR1 = AUX32[:, 2560:3072]
        SCR2 = AUX32[:, 3072:3584]
        NBM = AUX32[0:64, 2560:3072]
        fst = lambda par: XS[:, 0:2048] if par == 0 else AUX32[:, 0:2048]
        W3X = AUX[:, 4096:6144]
        OUT0 = XS[:, 0:2048]
        OUT1 = AUX32[:, 0:2048]

        bl_c = lambda qc, kt: BIA[:, qc * 12 + kt: qc * 12 + kt + 1]
        bg_c = lambda qc, kt: BIA[:, 24 + qc * 64 + kt: 24 + qc * 64 + kt + 1]

        def stc(pp, i):
            return STT[:, pp * 7 + i: pp * 7 + i + 1]
        EPSC = STT[:, 14:15]
        EPS8 = STT[:, 15:16]
        RG = lambda m: STT[:, 16 + m: 17 + m]
        ACOL = STT[:, 19:20]
        ASUM = lambda m: STT[:, 20 + m: 21 + m]

        pSA = E(nc.psum_tensor("pSA", [128, 1024], F32))
        pSB = E(nc.psum_tensor("pSB", [128, 1024], F32))
        pAVA = E(nc.psum_tensor("pAVA", [128, 512], F32))
        pAVB = E(nc.psum_tensor("pAVB", [128, 512], F32))
        pG = E(nc.psum_tensor("pG", [128, 512], F32))
        pT = E(nc.psum_tensor("pT", [128, 1024], BF16))

        sP = E(nc.semaphore("sP"))
        sV = E(nc.semaphore("sV"))
        sA = E(nc.semaphore("sA"))
        SEM = {"T": sP, "V": sV, "A": sA}
        for k in ("C", "X0", "X1", "CG0", "CG1", "W0", "W1", "W2", "W3",
                  "AB0", "AB1", "KV", "WO", "BO", "F0", "F1", "O0", "O1"):
            SEM[k] = E(nc.semaphore("s" + k))
        block = E(nc.Block())

        cnt = {k: 0 for k in SEM}
        q = {k: [] for k in ["T", "V", "A", "G", "Y"]}

        waited = {(qn, k): 0 for qn in q for k in SEM}

        def em(qn, fn, inc=None):
            if qn in ("T", "V", "A"):
                if qn in ("V", "A") and cnt[qn] > 0 and \
                        waited[(qn, qn)] < cnt[qn]:
                    waited[(qn, qn)] = cnt[qn]
                    q[qn].append(lambda e, sem=SEM[qn], tok=cnt[qn]:
                                 e.wait_ge(sem, tok))
                cnt[qn] += 1
                tok = cnt[qn]
                q[qn].append(lambda e, fn=fn, sem=SEM[qn]:
                             fn(e).then_inc(sem, 1))
                return tok
            tok = None
            if inc is not None:
                cnt[inc] += 16
                tok = cnt[inc]
                sem = SEM[inc]
                q[qn].append(lambda e, fn=fn, sem=sem:
                             fn(e).then_inc(sem, 16))
            else:
                q[qn].append(lambda e, fn=fn: fn(e))
            return tok

        def wt(qn, key, tok):
            if tok is None or tok <= 0:
                return
            if waited[(qn, key)] >= tok:
                return
            waited[(qn, key)] = tok
            sem = SEM[key]
            q[qn].append(lambda e, sem=sem, tok=tok: e.wait_ge(sem, tok))

        def tap(name, ap, cols):
            if not DEBUG:
                return
            fence()
            dt_ = nc.dram_tensor("dbg_" + name, [128, cols], BF16,
                                 kind="ExternalOutput")
            dbg_tensors[name] = dt_
            snapV = cnt["V"]
            snapT = cnt["T"]
            snapA = cnt["A"]
            wt("Y", "V", snapV)
            wt("Y", "T", snapT)
            wt("Y", "A", snapA)
            em("Y", lambda e, dt_=dt_, ap=ap: e.dma_start(dt_.ap(), ap), "C")

        def fence(queues=("T", "V", "A", "Y", "G")):
            snap = {k: cnt[k] for k in ("T", "V", "A")}
            for qn in queues:
                for k in ("T", "V", "A"):
                    wt(qn, k, snap[k])
            return snap

        em("Y", lambda e: e.dma_start(CL[:], d_cl.ap()), "C")
        em("Y", lambda e: e.dma_start(SL[:], d_sl.ap()), "C")
        em("Y", lambda e: e.dma_start(AUX[:, 3072:5120], d_msk.ap()), "C")
        em("Y", lambda e: e.dma_start(BIA[:, 0:24], d_bl.ap()), "C")
        em("Y", lambda e: e.dma_start(BIA[:, 24:152], d_bg.ap()), "C")
        em("Y", lambda e: e.dma_start(IDE[:], d_id.ap()), "C")
        em("Y", lambda e: e.dma_start(ID64[:], d_id64.ap()), "C")
        t_const = em("Y", lambda e: e.dma_start(RP[:], d_P.ap()), "C")
        em("V", lambda e: e.memset(ONE[:], 1.0))
        em("V", lambda e: e.memset(EPSC, EPS_LN))
        em("V", lambda e: e.memset(EPS8, EPS_Q))
        tv_ones = em("V", lambda e: e.memset(Vg3[:, :, 64:65], 1.0), "V")
        wt("T", "C", t_const)
        wt("A", "C", t_const)
        wt("V", "C", t_const)

        ABS_TILES = 128 * 3
        abs_state = {"i": 0, "vtoks": []}
        flatsrc = {0: d_w1, 1: d_w2, 2: d_w3}

        def pump_absum(n):
            for _ in range(n):
                i = abs_state["i"]
                if i >= ABS_TILES:
                    return
                m, j = divmod(i, 128)
                src = flatsrc[m].ap().rearrange("a b c -> (a b c)")
                seg = src[j * 128 * 704:(j + 1) * 128 * 704]
                seg = seg.rearrange("(p f) -> p f", p=128)
                par = i % 2
                if i >= 2:
                    wt("G", "V", abs_state["vtoks"][i - 2])
                gt = em("G", lambda e, seg=seg, par=par:
                        e.dma_start(absbuf(par), seg), "AB%d" % par)
                wt("V", "AB%d" % par, gt)
                em("V", lambda e, par=par: e.tensor_reduce(
                    ACOL, absbuf(par), AX.X, ALU.add,
                    apply_absolute_value=True))
                if j == 0:
                    vt = em("V", lambda e, m=m: e.tensor_copy(ASUM(m), ACOL),
                            "V")
                else:
                    vt = em("V", lambda e, m=m: e.tensor_tensor(
                        ASUM(m), ASUM(m), ACOL, ALU.add), "V")
                abs_state["vtoks"].append(vt)
                abs_state["i"] = i + 1

        ln_last = {"Aread": [0, 0], "Vnorm": [0, 0]}

        def ln_stats_only(pp, src_ap, junk_ap, src_tok, extra_a_wait=0):
            k, tok = src_tok
            wt("V", k, tok)
            wt("V", "A", ln_last["Aread"][pp])
            em("V", lambda e: e.tensor_reduce(stc(pp, 0), src_ap, AX.X,
                                              ALU.add))
            em("V", lambda e: e.tensor_scalar(
                stc(pp, 2), stc(pp, 0), 1.0 / D, None, ALU.mult))
            wt("A", k, tok)
            wt("A", "V", ln_last["Vnorm"][pp])
            wt("A", "V", extra_a_wait)
            ta = em("A", lambda e: e.activation(
                junk_ap, src_ap, AF.Square, accum_out=stc(pp, 1)), "A")
            wt("V", "A", ta)
            em("V", lambda e: e.tensor_scalar(
                stc(pp, 3), stc(pp, 2), stc(pp, 2), None, ALU.mult))
            tv2 = em("V", lambda e: e.scalar_tensor_tensor(
                stc(pp, 4), stc(pp, 1), 1.0 / D, stc(pp, 3),
                ALU.mult, ALU.subtract), "V")
            wt("A", "V", tv2)
            em("A", lambda e: e.activation(stc(pp, 5), stc(pp, 4), AF.Ln,
                                           bias=EPSC))
            ta2 = em("A", lambda e: e.activation(
                stc(pp, 6), stc(pp, 5), AF.Exp, scale=-0.5), "A")
            ln_last["Aread"][pp] = ta2
            return ta2

        def ln_tile(pp, src_ap, junk_ap, dst_ap, src_tok):
            ta2 = ln_stats_only(pp, src_ap, junk_ap, src_tok)
            wt("V", "A", ta2)
            tn = em("V", lambda e: e.tensor_scalar(
                dst_ap, src_ap, stc(pp, 2), stc(pp, 6),
                ALU.subtract, ALU.mult), "V")
            ln_last["Vnorm"][pp] = tn
            return tn

        state = {"evac": 0, "xg_war": 0, "xlt_war": 0, "rot_T": 0,
                 "xs_v": [0, 0], "xs_a": [0, 0], "cg_war": [0, 0]}

        def ln_transpose_tile(src_ap, src_tok, dst3, dst_col, pp):
            wt("V", "T", state["xlt_war"])
            tn = ln_tile(pp, src_ap, JUNK, XLT, src_tok)
            state["xs_v"][pp] = tn
            state["xs_a"][pp] = ln_last["Aread"][pp]
            tok = None
            for rnd in range(2):
                wt("T", "V", tn)
                wt("T", "V", state["evac"])
                for k in range(8):
                    dc = rnd * 8 + k
                    tok = em("T", lambda e, dc=dc, k=k: e.transpose(
                        pT[:, k * 128:(k + 1) * 128],
                        XLT[:, dc * 128:(dc + 1) * 128], IDE[:]),
                        "T" if k == 7 else None)
                wt("V", "T", tok)
                wt("V", "T", state["xg_war"])
                dst = dst3[:, rnd * 8:(rnd + 1) * 8, dst_col:dst_col + 128]
                state["evac"] = em("V", lambda e, dst=dst: e.tensor_copy(
                    dst, pT[:]), "V")
            state["xlt_war"] = tok

        xslot13 = [XS_M1, XSS]
        for tt in range(12):
            par = tt % 2
            wt("Y", "V", state["xs_v"][par])
            wt("Y", "A", state["xs_a"][par])
            ty = em("Y", lambda e, tt=tt, par=par: e.dma_start(
                xslot13[par], d_xhal.ap()[tt * 128:(tt + 1) * 128, :]),
                "X%d" % par)
            ln_transpose_tile(xslot13[par], ("X%d" % par, ty), xa3, tt * 128, par)
            pump_absum(2)

        tap("xln1", XA[:, 0:24576], 24576)
        wslot_tok = [0, 0, 0, 0]
        wslot_war = [0, 0, 0, 0]

        def wload(i, dram, p):
            si = i % 4
            wt("G", "T", wslot_war[si])
            wslot_tok[si] = em("G", lambda e, dram=dram, p=p, si=si:
                               e.dma_start(wslot(si), dram.ap()[p, :, :]),
                               "W%d" % si)
            return si

        PW = {k: ("V", 0) for k in
              ("pSA0", "pSA1", "pSB0", "pSB1", "pAVA", "pAVB", "pG")}

        def psum_wait(keys):
            for k in keys:
                ek, tok = PW[k]
                wt("T", ek, tok)

        def qkv_gemm(dst_psum_ap, pkey, wsl, c0, c1):
            psum_wait([pkey])
            tok = None
            for dc in range(NDC):
                tok = em("T", lambda e, dc=dc, wsl=wsl, c0=c0, c1=c1,
                         dst=dst_psum_ap:
                         e.matmul(dst, wslot(wsl)[:, dc * 128:(dc + 1) * 128],
                                  xln(dc, c0, c1),
                                  start=(dc == 0), stop=(dc == NDC - 1)),
                         "T" if dc == NDC - 1 else None)
            return tok

        def rope_chunk(psum_ap, pkey, cos_ap, sin_ap, dst_ap, ptok, W=512):
            wt("V", "T", ptok)
            wt("V", "T", state["rot_T"])
            t_raw = em("V", lambda e, ps=psum_ap, W=W: e.tensor_copy(
                RAW0[:, 0:W], ps), "V")
            wt("T", "V", t_raw)
            psum_wait(["pG"])
            t_rot = em("T", lambda e, W=W: e.matmul(
                pG[:, 0:W], RP[:], RAW0[:, 0:W], start=True, stop=True), "T")
            state["rot_T"] = t_rot
            em("V", lambda e, cos_ap=cos_ap, W=W: e.tensor_tensor(
                RTMP[:, 0:W], RAW0[:, 0:W], cos_ap, ALU.mult))
            wt("V", "T", t_rot)
            em("V", lambda e, sin_ap=sin_ap, W=W: e.tensor_tensor(
                RAW1[:, 0:W], pG[:, 0:W], sin_ap, ALU.mult))
            tok = em("V", lambda e, dst=dst_ap, W=W: e.tensor_tensor(
                dst, RTMP[:, 0:W], RAW1[:, 0:W], ALU.add), "V")
            PW[pkey] = ("V", tok)
            PW["pG"] = ("V", tok)
            return tok

        def attention(pair_idx, v_tile, n_ktp_of, bias_of, mask_of, kt_cols):
            pS = {0: ("pSA", pSA), 64: ("pSB", pSB)}
            pAV = {0: ("pAVA", pAVA), 64: ("pAVB", pAVB)}
            ready_v = cnt["V"]
            s_tok = {}
            e_tok = {}
            exp_war = {}
            last_exp = {0: None, 64: None}
            last_av = {}
            ep_tok = {0: 0, 64: 0}
            first_ever = True
            for qc in range(2):
                ktps = n_ktp_of(qc)
                nk = len(ktps)
                for i, kt in enumerate(ktps):
                    first, last = (i == 0), (i == nk - 1)
                    for hb in (0, 64):
                        pkey, pbuf = pS[hb]
                        prev = last_exp[hb]
                        if prev is not None:
                            wt("T", prev[1], prev[0])
                        if first_ever:
                            wt("T", "V", ready_v)
                            psum_wait([pkey + "0", pkey + "1"])
                        tok = None
                        for sub in range(2):
                            ktt = kt + sub
                            tok = em("T", lambda e, hb=hb, ktt=ktt, sub=sub,
                                     qc=qc, pbuf=pbuf:
                                     e.matmul(
                                         pbuf[:, sub * 512:(sub + 1) * 512],
                                         kt_cols(ktt, hb),
                                         Qp(qc * 512, qc * 512 + 512)[hb:hb + 64, :],
                                         start=True, stop=True,
                                         tile_position=(hb, 0)),
                                     "T" if sub == 1 else None)
                        s_tok[(hb, qc, i)] = tok
                    first_ever = False
                    for hb in (0, 64):
                        pkey, pbuf = pS[hb]
                        par = i % 2
                        wt("A", "T", s_tok[(hb, qc, i)])
                        wt("A", "T", last_av.get((hb, par), 0))
                        m = mask_of(qc, kt)
                        ta = em("A", lambda e, hb=hb, par=par, pbuf=pbuf,
                                b=bias_of(qc, kt):
                                e.activation(expsl(hb, 0, par), pbuf[:],
                                             AF.Exp, bias=b, scale=0.125),
                                "A")
                        PW[pkey + "0"] = ("A", ta)
                        PW[pkey + "1"] = ("A", ta)
                        if m is not None:
                            wt("V", "A", ta)
                            tv = em("V", lambda e, hb=hb, par=par, m=m:
                                    e.tensor_tensor(expsl(hb, 1, par),
                                                    expsl(hb, 0, par), m,
                                                    ALU.mult), "V")
                            e_tok[(hb, qc, i)] = (tv, "V", 1)
                            last_exp[hb] = (tv, "V")
                        else:
                            e_tok[(hb, qc, i)] = (ta, "A", 0)
                            last_exp[hb] = (ta, "A")
                    for hb in (0, 64):
                        akey, abuf = pAV[hb]
                        tok3 = e_tok[(hb, qc, i)]
                        kind = tok3[2]
                        par = i % 2
                        wt("T", tok3[1], tok3[0])
                        if first:
                            psum_wait([akey])
                        tok = None
                        for sub in range(2):
                            ktt = kt + sub
                            tok = em("T", lambda e, hb=hb, ktt=ktt, sub=sub,
                                     par=par, kind=kind, first=first,
                                     last=last, abuf=abuf:
                                     e.matmul(abuf[0:65, :],
                                              v_tile(ktt, hb),
                                              expsl(hb, kind, par)[:, sub * 512:(sub + 1) * 512],
                                              start=(first and sub == 0),
                                              stop=(last and sub == 1)),
                                     "T" if sub == 1 else None)
                        exp_war[(hb, qc, i)] = tok
                        last_av[(hb, par)] = tok
                for hb in (0, 64):
                    akey, abuf = pAV[hb]
                    avt = exp_war[(hb, qc, len(n_ktp_of(qc)) - 1)]
                    wt("A", "T", avt)
                    em("A", lambda e, abuf=abuf: e.activation(
                        SCR1[64:65, :], abuf[64:65, :], AF.Ln))
                    ta = em("A", lambda e: e.activation(
                        SCR2[64:65, :], SCR1[64:65, :], AF.Exp, scale=-1.0),
                        "A")
                    wt("T", "A", ta)
                    psum_wait(["pG"])
                    tb = em("T", lambda e: e.matmul(
                        pG[0:64, :], ONE[64:65, 0:64], SCR2[64:65, :],
                        start=True, stop=True, tile_position=(64, 0)), "T")
                    wt("V", "T", tb)
                    em("V", lambda e: e.tensor_copy(NBM[0:64, :], pG[0:64, :]))
                    tok = em("V", lambda e, hb=hb, qc=qc, p=pair_idx,
                             abuf=abuf:
                             e.tensor_tensor(
                                 attn_fm_h(p, hb, qc * 512, qc * 512 + 512),
                                 abuf[0:64, :], NBM[0:64, :], ALU.mult), "V")
                    ep_tok[hb] = tok
                    PW[akey] = ("V", tok)
                    PW["pG"] = ("V", tok)
                    wt("A", "T", tb)
            return ep_tok

        def l_mask_of(qc, kt):
            if qc == 0 and kt in (4, 6):
                return MSK2(kt - 4)
            if qc == 1 and kt in (8, 10):
                return MSK2(kt - 8)
            return None

        def l_nktp(qc):
            return [0, 2, 4, 6] if qc == 0 else [0, 2, 4, 6, 8, 10]

        order = []
        for p in range(NPAIR):
            order += [(p, d_wq), (p, d_wk), (p, d_wv)]
        for i in range(4):
            wload(i, order[i][1], order[i][0])
        next_w = 4
        wt("T", "V", tv_ones)
        t_attn_T = 0
        for p in range(NPAIR):
            sq, sk, sv = (3 * p) % 4, (3 * p + 1) % 4, (3 * p + 2) % 4
            wt("V", "T", t_attn_T)
            wt("T", "W%d" % sq, wslot_tok[sq])
            for half in range(2):
                ptok = qkv_gemm(pSA[:, half * 512:(half + 1) * 512],
                                "pSA%d" % half, sq,
                                512 + half * 512, 1024 + half * 512)
                c0 = 512 + half * 512
                rope_chunk(pSA[:, half * 512:(half + 1) * 512],
                           "pSA%d" % half,
                           CL[:, c0:c0 + 512], SL[:, c0:c0 + 512],
                           Qp(half * 512, half * 512 + 512), ptok)
            wslot_war[sq] = cnt["T"]
            wt("T", "W%d" % sk, wslot_tok[sk])
            kdst = [(pSB[:, 0:512], "pSB0"), (pSB[:, 512:1024], "pSB1"),
                    (pAVA[:], "pAVA")]
            for third in range(3):
                dstp, dkey = kdst[third]
                ptok = qkv_gemm(dstp, dkey, sk, third * 512, third * 512 + 512)
                c0 = third * 512
                rope_chunk(dstp, dkey, CL[:, c0:c0 + 512], SL[:, c0:c0 + 512],
                           Kp(third * 512, third * 512 + 512), ptok)
            wslot_war[sk] = cnt["T"]
            wt("T", "W%d" % sv, wslot_tok[sv])
            tvf = 0
            for third in range(3):
                ptok = qkv_gemm(pAVB[:], "pAVB", sv,
                                third * 512, third * 512 + 512)
                wt("V", "T", ptok)
                tvf = em("V", lambda e, third=third: e.tensor_copy(
                    VFM[:, third * 512:(third + 1) * 512], pAVB[:]), "V")
                PW["pAVB"] = ("V", tvf)
            wslot_war[sv] = cnt["T"]
            while next_w < len(order) and next_w <= 3 * p + 6:
                wload(next_w, order[next_w][1], order[next_w][0])
                next_w += 1
            wt("T", "V", tvf)
            for kt0, ncnt in ((0, 8), (8, 4)):
                wt("T", "V", state["evac"])
                tok = None
                for k in range(ncnt):
                    kt = kt0 + k
                    tok = em("T", lambda e, kt=kt, k=k: e.transpose(
                        pT[:, k * 128:(k + 1) * 128],
                        VFM[:, kt * 128:(kt + 1) * 128], IDE[:]),
                        "T" if k == ncnt - 1 else None)
                wt("V", "T", tok)
                for h in (0, 1):
                    off = VG0 + (2 * kt0 + h) * 65
                    seg = AB[:, off:off + ncnt * 130]
                    dst = seg.rearrange("p (k c) -> p k c", c=130)[:, :, 0:64]
                    src = pT[:, 0:ncnt * 128].rearrange(
                        "p (k d) -> p k d", d=128)[:, :, h * 64:(h + 1) * 64]
                    state["evac"] = em("V", lambda e, dst=dst, src=src:
                                       e.tensor_copy(dst, src), "V")
            pump_absum(7)
            if p == 0:
                tap("qkv0", MID[:, 8192:10752], 2560)
                tap("vfm0", MID[:, 12800:14336], 1536)
                tap("vaug0", AB[:, VG0:VG0 + 24 * 65], 24 * 65)
            ep = attention(
                p,
                lambda kt, hb: Vg_t(kt * 2 + (hb // 64)),
                l_nktp,
                lambda qc, kt: bl_c(qc, kt),
                l_mask_of,
                lambda kt, hb: Kp(kt * 128, kt * 128 + 128)[hb:hb + 64, :])
            t_attn_T = cnt["T"]

        tap("attnloc", AB[:, 0:15 * 1024], 15 * 1024)
        fence()
        s15 = next_w % 4
        wload(next_w, d_wq, 15)
        next_w += 1
        em("G", lambda e: e.dma_start(kvgw(0), d_wk.ap()[15, :, :]), "KV")
        tvw = em("G", lambda e: e.dma_start(kvgw(1), d_wv.ap()[15, :, :]), "KV")
        tkw = tvw
        wt("T", "W%d" % s15, wslot_tok[s15])
        for half in range(2):
            ptok = qkv_gemm(pSA[:, half * 512:(half + 1) * 512],
                            "pSA%d" % half, s15,
                            512 + half * 512, 1024 + half * 512)
            c0 = 512 + half * 512
            rope_chunk(pSA[:, half * 512:(half + 1) * 512], "pSA%d" % half,
                       CL[:, c0:c0 + 512], SL[:, c0:c0 + 512],
                       Qp(half * 512, half * 512 + 512), ptok)
        wslot_war[s15] = cnt["T"]
        state["cg_war"] = [cnt["V"], cnt["V"]]

        wt("T", "KV", tvw)
        for sc in range(32):
            par = sc % 2
            wt("Y", "V", state["cg_war"][par])
            em("Y", lambda e, sc=sc, par=par: e.dma_start(
                cgs(par), d_cg.ap()[:, sc * 256:(sc + 1) * 256]),
                "CG%d" % par)
            tsg = em("Y", lambda e, sc=sc, par=par: e.dma_start(
                sgs(par), d_sg.ap()[:, sc * 256:(sc + 1) * 256]),
                "CG%d" % par)
            for t4 in range(2):
                tt = sc * 2 + t4
                xpar = tt % 2
                wt("Y", "V", state["xs_v"][xpar])
                wt("Y", "A", state["xs_a"][xpar])
                ty = em("Y", lambda e, tt=tt, xpar=xpar: e.dma_start(
                    xslot13[xpar], d_xg.ap()[tt * 128:(tt + 1) * 128, :]),
                    "X%d" % xpar)
                ln_transpose_tile(xslot13[xpar], ("X%d" % xpar, ty), xg3,
                                  t4 * 128, xpar)
            psum_wait(["pSA0"])
            wt("T", "V", state["evac"])
            tokk = None
            for dc in range(NDC):
                tokk = em("T", lambda e, dc=dc: e.matmul(
                    pSA[:, 0:256], kvgw(0)[:, dc * 128:(dc + 1) * 128],
                    xg_fm(dc), start=(dc == 0), stop=(dc == NDC - 1)),
                    "T" if dc == NDC - 1 else None)
            wt("V", "CG%d" % par, tsg)
            state["cg_war"][par] = rope_chunk(
                pSA[:, 0:256], "pSA0", cgs(par), sgs(par),
                Kg(sc * 256, (sc + 1) * 256), tokk, W=256)
            psum_wait(["pSB0"])
            tokv = None
            for dc in range(NDC):
                tokv = em("T", lambda e, dc=dc: e.matmul(
                    pSB[:, 0:256], kvgw(1)[:, dc * 128:(dc + 1) * 128],
                    xg_fm(dc), start=(dc == 0), stop=(dc == NDC - 1)),
                    "T" if dc == NDC - 1 else None)
            state["xg_war"] = tokv
            wt("V", "T", tokv)
            tvf = em("V", lambda e: e.tensor_copy(
                VFM[:, 0:256], pSB[:, 0:256]), "V")
            PW["pSB0"] = ("V", tvf)
            wt("T", "V", tvf)
            wt("T", "V", state["evac"])
            tok = None
            for k in range(2):
                tok = em("T", lambda e, k=k: e.transpose(
                    pT[:, k * 128:(k + 1) * 128],
                    VFM[:, k * 128:(k + 1) * 128], IDE[:]),
                    "T" if k == 1 else None)
            wt("V", "T", tok)
            for h in (0, 1):
                off = VG0 + (2 * (sc * 2) + h) * 65
                seg = AB[:, off:off + 2 * 130]
                dst = seg.rearrange("p (k c) -> p k c", c=130)[:, :, 0:64]
                src = pT[:, 0:256].rearrange(
                    "p (k d) -> p k d", d=128)[:, :, h * 64:(h + 1) * 64]
                state["evac"] = em("V", lambda e, dst=dst, src=src:
                                   e.tensor_copy(dst, src), "V")
            pump_absum(8)

        tap("kg", AB[:, KG0:KG0 + 8192], 8192)
        tap("vg", AB[:, VG0:VG0 + 128 * 65], 128 * 65)
        tap("qg", MID[:, QP0:QP0 + 1024], 1024)
        fence()

        def g_nktp(qc):
            if qc == 0:
                return [0, 2] + list(range(8, 64, 2))
            return list(range(0, 64, 2))

        def g_mask_of(qc, kt):
            if qc == 0 and kt in (0, 2):
                return MSK2(kt)
            if qc == 1 and kt in (4, 6):
                return MSK2(kt - 4)
            return None

        ep = attention(
            15,
            lambda kt, hb: Vg_t(kt * 2 + (hb // 64)),
            g_nktp,
            lambda qc, kt: bg_c(qc, kt),
            g_mask_of,
            lambda kt, hb: Kg(kt * 128, kt * 128 + 128)[hb:hb + 64, :])
        t_attn_done_V = max(ep[0], ep[64])
        t3b_T = cnt["T"]
        t3b_A = cnt["A"]

        pump_absum(ABS_TILES)
        t_asum_v = cnt["V"]
        NTOT = float(FF * D)
        for m in range(3):
            wt("T", "V", t_asum_v)
            psum_wait(["pG"])
            tg = em("T", lambda e, m=m: e.matmul(
                pG[0:1, 0:1], ASUM(m), ONE[:, 0:1],
                start=True, stop=True), "T")
            wt("A", "T", tg)
            em("A", lambda e: e.activation(
                SCR1[0:1, 0:1], pG[0:1, 0:1], AF.Ln, scale=1.0 / NTOT,
                bias=EPS8[0:1, :]))
            ta = em("A", lambda e: e.activation(
                SCR2[0:1, 0:1], SCR1[0:1, 0:1], AF.Exp, scale=-1.0), "A")
            PW["pG"] = ("A", ta)
            wt("T", "A", ta)
            psum_wait(["pG"])
            em("T", lambda e: e.matmul(
                pG[0:64, 1:2], ONE[0:1, 0:64], SCR2[0:1, 0:1],
                start=True, stop=True))
            tb = em("T", lambda e: e.matmul(
                pG[64:128, 1:2], ONE[0:1, 0:64], SCR2[0:1, 0:1],
                start=True, stop=True, tile_position=(0, 64)), "T")
            wt("V", "T", tb)
            tvr = em("V", lambda e, m=m: e.tensor_copy(RG(m), pG[:, 1:2]),
                     "V")
            PW["pG"] = ("V", tvr)
        t_rg_v = cnt["V"]

        fence()
        wo_tok = [0] * 16
        wt("G", "T", t3b_T)
        for i in range(16):
            dstw = wo_ab(i) if i < 13 else wo_w(i)
            wo_tok[i] = em("G", lambda e, i=i, dstw=dstw: e.dma_start(
                dstw, d_wo.ap()[i, :, :]), "WO")
        t_bo = em("Y", lambda e: e.dma_start(BOBC, d_bo.ap()), "BO")

        def wo_t(i):
            return wo_ab(i) if i < 13 else wo_w(i)

        pbank4 = [(pSA[:, 0:512], "pSA0"), (pSA[:, 512:1024], "pSA1"),
                  (pSB[:, 0:512], "pSB0"), (pSB[:, 512:1024], "pSB1")]
        xslot4 = [XS_M4, XSS]
        for tc in range(8):
            xpar = tc % 2
            wt("Y", "V", state["xs_v"][xpar])
            wt("Y", "A", state["xs_a"][xpar])
            tyx = em("Y", lambda e, tc=tc, xpar=xpar: e.dma_start(
                xslot4[xpar], d_xg.ap()[tc * 128:(tc + 1) * 128, :]),
                "X%d" % xpar)
            for hs in range(4):
                pb, pk = pbank4[hs]
                psum_wait([pk])
                wt("T", "V", t_attn_done_V)
                tok = None
                wt("T", "WO", wo_tok[15])
                for pc in range(16):
                    tok = em("T", lambda e, pc=pc, tc=tc, hs=hs, pb=pb:
                             e.matmul(pb,
                                      attn_fm(pc)[:, tc * 128:(tc + 1) * 128],
                                      wo_t(pc)[:, hs * 512:(hs + 1) * 512],
                                      start=(pc == 0), stop=(pc == 15)),
                             "T" if pc == 15 else None)
                wt("V", "T", tok)
                wt("V", "BO", t_bo)
                tev = em("V", lambda e, hs=hs, pb=pb: e.tensor_tensor(
                    APROJ[:, hs * 512:(hs + 1) * 512], pb,
                    BOBC[:, hs * 512:(hs + 1) * 512], ALU.add), "V")
                PW[pk] = ("V", tev)
            ta2 = ln_stats_only(xpar, APROJ, JUNK5, ("V", tev))
            wt("V", "A", ta2)
            em("V", lambda e, pp=xpar: e.tensor_scalar(
                TMPF, APROJ, stc(pp, 2), stc(pp, 6), ALU.subtract, ALU.mult))
            ln_last["Vnorm"][xpar] = cnt["V"]
            wt("V", "X%d" % xpar, tyx)
            tr1 = em("V", lambda e, tc=tc, xpar=xpar: e.tensor_tensor(
                r1t(tc), TMPF, xslot4[xpar], ALU.add), "V")
            state["xs_v"][xpar] = tr1
        t4_V = cnt["V"]
        t4_T = cnt["T"]

        tap("attng", AB[:, 15 * 1024:16 * 1024], 1024)
        tap("r1", XA[:, 0:16384], 16384)
        fence()
        for tc in range(8):
            pp = tc % 2
            wt("V", "T", state["xlt_war"])
            tn = ln_tile(pp, r1t(tc), JUNK5, XLT5, ("V", t4_V))
            tok = None
            for rnd in range(2):
                wt("T", "V", tn)
                wt("T", "V", state["evac"])
                for k in range(8):
                    dc = rnd * 8 + k
                    tok = em("T", lambda e, dc=dc, k=k: e.transpose(
                        pT[:, k * 128:(k + 1) * 128],
                        XLT5[:, dc * 128:(dc + 1) * 128], IDE[:]),
                        "T" if k == 7 else None)
                wt("V", "T", tok)
                dst = mid3[:, rnd * 8:(rnd + 1) * 8, tc * 128:(tc + 1) * 128]
                state["evac"] = em("V", lambda e, dst=dst: e.tensor_copy(
                    dst, pT[:]), "V")
            state["xlt_war"] = tok
        t_xln2_V = cnt["V"]

        f_tok = [0, 0]
        f_war = [t_attn_done_V, t_attn_done_V]
        q_war = {0: t4_T, 1: t4_T}
        sig_war = [0, 0]
        hid_war = [0, 0]
        hid_tok = [0] * NFC
        wt("Y", "V", t4_V)
        wt("Y", "A", cnt["A"])
        wt("Y", "T", t4_T)

        def fload(li, dram, j):
            par = li % 2
            wt("Y", "V", f_war[par])
            f_tok[par] = em("Y", lambda e, dram=dram, j=j, par=par:
                            e.dma_start(fst(par), dram.ap()[j, :, :]),
                            "F%d" % par)
            return par

        fl = {(0, 0): fload(0, d_w1, 0), (0, 1): fload(1, d_w2, 0)}
        li = 2
        for j in range(NFC):
            for m in (0, 1):
                par = fl.pop((j, m))
                qpar = j % 2
                wt("V", "F%d" % par, f_tok[par])
                wt("V", "T", q_war[m])
                em("V", lambda e, par=par, m=m: e.tensor_scalar(
                    QT1, fst(par), RG(m), 0.5, ALU.mult, ALU.is_ge))
                em("V", lambda e, par=par, m=m: e.tensor_scalar(
                    QT2, fst(par), RG(m), -0.5, ALU.mult, ALU.is_ge))
                tq = em("V", lambda e, m=m, qpar=qpar: e.scalar_tensor_tensor(
                    w12q(m, qpar), QT1, -1.0, QT2, ALU.add, ALU.add), "V")
                f_war[par] = tq
                if li < 2 * NFC:
                    jj, mm = divmod(li, 2)
                    fl[(jj, mm)] = fload(li, d_w1 if mm == 0 else d_w2, jj)
                    li += 1
                pbank, pkeys = ((pSA, ("pSA0", "pSA1")) if m == 0
                                else (pSB, ("pSB0", "pSB1")))
                wt("T", "V", tq)
                wt("T", "V", t_xln2_V)
                for th in range(2):
                    psum_wait([pkeys[th]])
                    tok = None
                    for dc in range(NDC):
                        tok = em("T", lambda e, dc=dc, m=m, th=th, qpar=qpar,
                                 pbank=pbank:
                                 e.matmul(pbank[:, th * 512:(th + 1) * 512],
                                          w12q(m, qpar)[:, dc * 128:(dc + 1) * 128],
                                          xln2(dc, th * 512, (th + 1) * 512),
                                          start=(dc == 0),
                                          stop=(dc == NDC - 1)),
                                 "T" if dc == NDC - 1 else None)
                    if m == 0:
                        wt("A", "T", tok)
                        sig_war[th] = em("A", lambda e, th=th: e.activation(
                            SG(th), pSA[:, th * 512:(th + 1) * 512],
                            AF.Sigmoid), "A")
                        PW[pkeys[th]] = ("A", sig_war[th])
                    else:
                        wt("V", "T", tok)
                        wt("V", "A", sig_war[th])
                        hid_war[th] = em("V", lambda e, j=j, th=th:
                                         e.tensor_tensor(
                                             hid(j)[:, th * 512:(th + 1) * 512],
                                             pSB[:, th * 512:(th + 1) * 512],
                                             SG(th), ALU.mult), "V")
                        PW[pkeys[th]] = ("V", hid_war[th])
                q_war[m] = cnt["T"]
            hid_tok[j] = hid_war[1]
        t5_T = cnt["T"]
        t5_A = cnt["A"]
        t5ln_A = max(ln_last["Aread"])

        tap("xln2", MID[:, 0:16384], 16384)
        tap("hid01", AB[:, 0:8192], 8192)
        fence()
        groups = [(5 * i, min(5 * i + 5, 44)) for i in range(9)]

        def w3slot(g):
            if g < 4:
                return wqs_w3(g)
            return W3X

        w3_war = {g: t5_T for g in range(5)}
        out_war = [0, 0]
        out_toks = []
        f_war6 = [max(f_war), max(f_war)]
        f_tok6 = [0, 0]
        li = 0
        acc_first_wait = t5_T
        for gi, (j0, j1) in enumerate(groups):
            glast = (gi == len(groups) - 1)
            for g, j in enumerate(range(j0, j1)):
                par = li % 2
                wt("Y", "V", f_war6[par])
                f_tok6[par] = em("Y", lambda e, j=j, par=par: e.dma_start(
                    fst(par), d_w3.ap()[j, :, :]), "F%d" % par)
                li += 1
                wt("V", "F%d" % par, f_tok6[par])
                wt("V", "T", w3_war[g])
                wt("V", "A", t5ln_A)
                em("V", lambda e, par=par: e.tensor_scalar(
                    QT1, fst(par), RG(2), 0.5, ALU.mult, ALU.is_ge))
                em("V", lambda e, par=par: e.tensor_scalar(
                    QT2, fst(par), RG(2), -0.5, ALU.mult, ALU.is_ge))
                tq = em("V", lambda e, g=g: e.scalar_tensor_tensor(
                    w3slot(g), QT1, -1.0, QT2, ALU.add, ALU.add), "V")
                f_war6[par] = tq
            t_q6 = cnt["V"]
            if gi == 0:
                tap("w3q0", WQS[:, 0:8192], 8192)
                tap("w3x0", AUX[:, 4096:6144], 2048)
            if gi == 1:
                tap("accg0", MID[:, 0:16384], 16384)
                tap("hidg1", AB[:, 0:8192], 8192)
                tap("w3qg1", WQS[:, 0:8192], 8192)
            for tc in range(8):
                for hs in range(4):
                    pb, pk = pbank4[hs]
                    psum_wait([pk])
                    wt("T", "V", t_q6)
                    tok = None
                    for g, j in enumerate(range(j0, j1)):
                        wt("T", "V", hid_tok[j])
                        tok = em("T", lambda e, g=g, j=j, tc=tc, hs=hs,
                                 pb=pb, last=(j == j1 - 1):
                                 e.matmul(pb,
                                          hid(j)[:, tc * 128:(tc + 1) * 128],
                                          w3slot(g)[:, hs * 512:(hs + 1) * 512],
                                          start=(g == 0), stop=last),
                                 "T" if j == j1 - 1 else None)
                    wt("V", "T", tok)
                    if gi == 0:
                        tev = em("V", lambda e, tc=tc, hs=hs, pb=pb:
                                 e.tensor_copy(
                                     acct(tc)[:, hs * 512:(hs + 1) * 512],
                                     pb), "V")
                    else:
                        em("V", lambda e, tc=tc, hs=hs, pb=pb:
                           e.tensor_tensor(
                               TMP6, pb,
                               acct(tc)[:, hs * 512:(hs + 1) * 512], ALU.add))
                        tev = em("V", lambda e, tc=tc, hs=hs: e.tensor_copy(
                            acct(tc)[:, hs * 512:(hs + 1) * 512], TMP6), "V")
                    PW[pk] = ("V", tev)
                if glast:
                    pp = tc % 2
                    opar = tc % 2
                    tacc = cnt["V"]
                    ta2 = ln_stats_only(pp, acct(tc), JUNK6, ("V", tacc),
                                        extra_a_wait=t_q6)
                    wt("V", "A", ta2)
                    em("V", lambda e, tc=tc, pp=pp: e.tensor_scalar(
                        TMPF6, acct(tc), stc(pp, 2), stc(pp, 6),
                        ALU.subtract, ALU.mult))
                    ln_last["Vnorm"][pp] = cnt["V"]
                    wt("V", "O%d" % opar, out_war[opar])
                    outb = OUT0 if opar == 0 else OUT1
                    tout = em("V", lambda e, tc=tc, outb=outb: e.tensor_tensor(
                        outb, TMPF6, r1t(tc), ALU.add), "V")
                    wt("Y", "V", tout)
                    ot = em("Y", lambda e, tc=tc, outb=outb: e.dma_start(
                        d_y.ap()[tc * 128:(tc + 1) * 128, :], outb),
                        "O%d" % opar)
                    out_war[opar] = ot
                    out_toks.append(ot)
            for g in range(5):
                w3_war[g] = cnt["T"]

        tap("acc", MID[:, 0:16384], 16384)
        for k in SEM:
            if k not in ("T", "V", "A"):
                wt("Y", k, cnt[k])

        @block.tensor
        def _(t):
            for fn in q["T"]:
                fn(t)

        @block.vector
        def _(v):
            for fn in q["V"]:
                fn(v)

        @block.scalar
        def _(s):
            for fn in q["A"]:
                fn(s)

        @block.gpsimd
        def _(g):
            for fn in q["G"]:
                fn(g)

        @block.sync
        def _(sy):
            for fn in q["Y"]:
                fn(sy)

    return nc


def _rope_tables(pos):
    inv = 1.0 / (10000.0 ** (np.arange(0, 64, 2, dtype=np.float64) / 64.0))
    ang = pos[None, :].astype(np.float64) * inv[:, None]
    ang64 = np.concatenate([ang, ang], 0)
    c = np.cos(ang64)
    s = np.sin(ang64)
    c = np.concatenate([c, c], 0).astype(np.float32)
    s = np.concatenate([s, s], 0).astype(np.float32)
    return c.astype(BF16NP), s.astype(BF16NP)


def kernel(L0_attn_hidden_states, L0_attn_Wq, L0_attn_Wk, L0_attn_Wv,
           L0_attn_Wo, L0_attn_bo, L0_ffn_w1, L0_ffn_w2, L0_ffn_w3):
    x = np.ascontiguousarray(np.asarray(L0_attn_hidden_states,
                                        dtype=np.float32)[0])
    Wq = np.asarray(L0_attn_Wq, np.float32)
    Wk = np.asarray(L0_attn_Wk, np.float32)
    Wv = np.asarray(L0_attn_Wv, np.float32)
    Wo = np.asarray(L0_attn_Wo, np.float32)
    bo = np.asarray(L0_attn_bo, np.float32)
    w1 = np.asarray(L0_ffn_w1, np.float32)
    w2 = np.asarray(L0_ffn_w2, np.float32)
    w3 = np.asarray(L0_ffn_w3, np.float32)

    def qkv_tiles(W):
        t = W.reshape(16, 128, 16, 128).transpose(0, 3, 2, 1)
        return np.ascontiguousarray(t).reshape(16, 128, 2048)

    wq_t = qkv_tiles(Wq)
    wk_t = qkv_tiles(Wk)
    wv_t = qkv_tiles(Wv)
    wo_t = np.ascontiguousarray(Wo.T.reshape(16, 128, 2048))
    bo_bc = np.ascontiguousarray(np.broadcast_to(bo, (128, 2048)))

    def ffn_tiles(w):
        t = w.reshape(44, 128, 16, 128).transpose(0, 3, 2, 1)
        return np.ascontiguousarray(t).reshape(44, 128, 2048)

    w1_t = ffn_tiles(w1)
    w2_t = ffn_tiles(w2)
    w3_t = np.ascontiguousarray(w3.T.reshape(44, 128, 2048))

    r = np.arange(128)[:, None]
    jj = np.arange(512)[None, :]
    masks = np.concatenate(
        [(r + o <= jj) for o in (0, 128, 256, 384)], axis=1
    ).astype(BF16NP)

    identb = np.eye(128, dtype=BF16NP)
    id64 = np.zeros((128, 64), np.float32)
    for i in range(128):
        id64[i, i % 64] = 1.0
    id64 = id64.astype(BF16NP)
    P = np.zeros((128, 128), np.float32)
    for blk in (0, 64):
        for i in range(32):
            P[blk + i + 32, blk + i] = -1.0
            P[blk + i, blk + i + 32] = 1.0
    ropeP = P.astype(BF16NP)

    in_maps = []
    for c in range(NCORES):
        start = c * T
        xh = np.zeros((1536, D), np.float32)
        lo = start - 512
        src_lo = max(0, lo)
        xh[src_lo - lo:, :] = x[src_lo:start + T]
        xg = np.concatenate([x[start:start + T], x[:start], x[start + T:]], 0)
        cl, sl = _rope_tables(np.arange(start - 512, start + T))
        pos_g = np.concatenate([np.arange(start, start + T),
                                np.arange(0, start),
                                np.arange(start + T, S)])
        cg, sg = _rope_tables(pos_g)
        bl = np.full((128, 24), -CSH, np.float32)
        if c == 0:
            for qc in range(2):
                bl[:, qc * 12 + 0: qc * 12 + 4] = NEG
        bg = np.full((128, 128), -CSH, np.float32)
        for qc in range(2):
            for kt in range(64):
                g = kt // 8
                if g > c:
                    bg[:, qc * 64 + kt] = NEG
        in_maps.append({
            "x_hal": np.ascontiguousarray(xh),
            "x_glob": np.ascontiguousarray(xg),
            "wq_t": wq_t, "wk_t": wk_t, "wv_t": wv_t, "wo_t": wo_t,
            "bo_bc": bo_bc, "w1_t": w1_t, "w2_t": w2_t, "w3_t": w3_t,
            "cos_loc": np.ascontiguousarray(cl),
            "sin_loc": np.ascontiguousarray(sl),
            "cos_glob": np.ascontiguousarray(cg),
            "sin_glob": np.ascontiguousarray(sg),
            "masks": masks, "bias_loc": bl, "bias_glob": bg,
            "identb": identb, "id64": id64, "ropeP": ropeP,
        })

    nc = build_nc()
    res = run_bass_kernel_spmd(nc, in_maps, core_ids=list(range(NCORES)))
    y = np.concatenate([res.results[c]["y"] for c in range(NCORES)], 0)
    return y[None].astype(np.float32)
